# version 02 (final)
# baseline (speedup 1.0000x reference)
import os
import numpy as np
import ml_dtypes
from contextlib import ExitStack
import concourse.bass as bass
import concourse.mybir as mybir
from concourse.bass_utils import run_bass_kernel_spmd

F32 = mybir.dt.float32
BF16 = mybir.dt.bfloat16
I32 = mybir.dt.int32
U16 = mybir.dt.uint16
AF = mybir.ActivationFunctionType
ALU = mybir.AluOpType
BF = ml_dtypes.bfloat16

D = 4096
PROJ = 18560
INTER = 8192
CONVC = 10240
H, PH, NST, G, KCV, CS = 128, 64, 128, 8, 4, 256
T = 1024
TH = T + 3
NCH = 4
NCORES = 8
EPS = 1e-5
OT_DT = 0
OT_B = 1
OT_C = 9
OT_HS = 17
OT_GATE = 81
NOT_A = 81
NOT = 145

PROC_TILES = [144] + list(range(128, 136)) + list(range(136, 144)) \
    + list(range(64, 128)) + list(range(0, 64))

MASK_THRESH = 4096.00048828125



def _prep_shared(inp):
    out = {}
    w = np.asarray(inp["L0_attn_in_proj_w"])
    wta = np.empty((NOT, 128, 32, 128), np.float32)
    wperm = w.reshape(145, 128, 32, 128)
    for t, tile in enumerate(PROC_TILES):
        wta[t] = wperm[tile].transpose(2, 1, 0)
    out["wTa"] = np.ascontiguousarray(wta)

    wo = np.asarray(inp["L0_attn_out_w"])
    wto = wo.reshape(16, 256, 64, 128).transpose(0, 3, 2, 1)
    out["wTo"] = np.ascontiguousarray(wto)

    out["identb"] = np.eye(128, dtype=BF)
    out["identf"] = np.eye(128, dtype=np.float32)
    out["ones1"] = np.ones((1, 128), dtype=BF)
    out["zer"] = np.zeros((128, INTER), dtype=BF)
    out["plv"] = np.broadcast_to(np.arange(8, dtype=np.int32)[None, :], (128, 8)).copy()
    selh = np.zeros((128, 128, 128), BF)
    for h in range(128):
        selh[h, h, :] = 1
    out["selh"] = selh
    tri = np.zeros((2, 128, 256), np.float32)
    for half in range(2):
        for p in range(128):
            tri[half, p, 128 * half + p:] = 1
    out["tri01"] = tri
    bsel = np.zeros((8, 128, 8), BF)
    for g in range(8):
        bsel[g, :, g] = 1
    out["Bsel8"] = bsel
    g8 = np.zeros((8, 8, 128), np.float32)
    for gg in range(8):
        g8[gg, gg, :] = 1
    out["g8sel"] = g8
    out["shv"] = (np.arange(128, dtype=np.int32)[:, None] % 16) + \
        16 * np.arange(2, dtype=np.int32)[None, :]
    r16 = (np.arange(64) // 4).astype(np.int32)
    out["off8"] = np.broadcast_to((256 * r16)[None, :], (128, 64)).copy().astype(np.int32)
    out["off4"] = np.broadcast_to((4096 + 16 * r16)[None, :], (128, 64)).copy().astype(np.int32)

    out["dtb"] = np.asarray(inp["L0_attn_dt_bias"], np.float32).reshape(128, 1)
    out["alog"] = np.asarray(inp["L0_attn_A_log"], np.float32).reshape(128, 1)
    Dp = np.asarray(inp["L0_attn_D"], np.float32)
    drepT = np.empty((128, 64), np.float32)
    for m in range(64):
        drepT[0:64, m] = Dp[2 * m]
        drepT[64:128, m] = Dp[2 * m + 1]
    out["drepT"] = drepT
    cw = np.asarray(inp["L0_attn_conv_w"], np.float32).reshape(10240, 4)
    out["cw"] = np.ascontiguousarray(cw.reshape(80, 128, 4).transpose(1, 0, 2))
    cb = np.asarray(inp["L0_attn_conv_b"], np.float32)
    out["cb"] = np.ascontiguousarray(cb.reshape(80, 128).T)
    nwv = np.asarray(inp["L0_attn_norm_w"], np.float32)
    out["nw"] = np.ascontiguousarray(nwv.reshape(64, 128).T)
    fbv = np.asarray(inp["L0_ffn_bias"], np.float32)
    out["fb"] = np.ascontiguousarray(np.broadcast_to(fbv[None, :], (128, 4096)))
    return out


def _prep_core(inp, r):
    out = {}
    x = np.asarray(inp["L0_attn_input_states"], np.float32)
    row, pos = r // 4, r % 4
    s0 = pos * 1024
    xh = np.zeros((TH, D), np.float32)
    if pos > 0:
        xh[0:3] = x[row, s0 - 3:s0]
    xh[3:] = x[row, s0:s0 + 1024]
    out["xh"] = xh

    qw = np.asarray(inp["L0_ffn_qweight"])
    rowbase = 512 * r
    qs = qw[:, rowbase:rowbase + 512, :]
    qs = qs.reshape(8, 4, 8, 16, 4, 32)
    qs = qs.transpose(0, 1, 2, 3, 4, 5)
    qrep = np.empty((8, 4, 128, 2048), np.int32)
    for kk in range(8):
        for m in range(16):
            qrep[:, :, 16 * kk + m, :] = qs[:, :, kk].reshape(8, 4, 16 * 4 * 32)
    out["qrep"] = qrep

    lut8 = np.asarray(inp["L0_ffn_lut8"], np.float32)
    lut4 = np.asarray(inp["L0_ffn_lut4"], np.float32)
    ltbl = np.empty((4, 128, 4352), np.float32)
    for tile in range(4):
        for p in range(128):
            kk = p // 16
            rows = rowbase + 128 * tile + 16 * kk + np.arange(16)
            ltbl[tile, p, :4096] = lut8[rows].reshape(-1)
            ltbl[tile, p, 4096:] = lut4[rows].reshape(-1)
    out["ltbl"] = ltbl

    selAe = np.zeros((1, 8, 128), np.float32)
    selBf = np.zeros((1, 8, 8, 128), np.float32)
    lo = row * 4
    for m in range(lo, r):
        selAe[0, m, :] = 1.0
        for j in range(m + 1, r):
            selBf[0, j, m, :] = 1.0
    out["selAe"] = selAe.reshape(1, 1024)
    out["selBf"] = selBf.reshape(1, 8192)
    return out



class Sem:
    def __init__(self, ctx, nc, name):
        self.h = ctx.enter_context(nc.semaphore(name))
        self.n = 0

    def inc(self, inst, k=1):
        inst.then_inc(self.h, k)
        self.n += k
        return self.n

    def dma(self, inst):
        return self.inc(inst, 16)

    def wait(self, eng, val=None):
        eng.wait_ge(self.h, self.n if val is None else val)




class Prog:

    def __init__(self, stage, dbg=(), upto=99):
        self.stage = stage
        self.upto = upto
        self.dbg = set(dbg)
        self.nc = bass.Bass("TRN2", target_bir_lowering=False)
        self.ctx = ExitStack()

    def sb(self, name, shape, dt):
        return self.ctx.enter_context(self.nc.sbuf_tensor(name, shape, dt))

    def ps(self, name, shape, dt=F32):
        return self.ctx.enter_context(self.nc.psum_tensor(name, shape, dt))

    def sem(self, name):
        return Sem(self.ctx, self.nc, name)

    def din(self, name, shape, dt):
        return self.nc.dram_tensor(name, shape, dt, kind="ExternalInput").ap()

    def dout(self, name, shape, dt):
        return self.nc.dram_tensor(name, shape, dt, kind="ExternalOutput").ap()

    def dio(self, name, shape, dt):
        if self.stage == "fused":
            return self.nc.dram_tensor(name, shape, dt).ap()
        return (self.dout if self.stage == "l1" else self.din)(name, shape, dt)

    def build(self):
        st = self.stage
        d = self.d = {}
        d["xh"] = self.din("xh", [TH, D], F32)
        d["identb"] = self.din("identb", [128, 128], BF16)
        d["identf"] = self.din("identf", [128, 128], F32)
        d["ones1"] = self.din("ones1", [1, 128], BF16)
        d["dtb"] = self.din("dtb", [128, 1], F32)
        d["zer"] = self.din("zer", [128, INTER], BF16)
        d["alog"] = self.din("alog", [128, 1], F32)
        if st != "l2":
            d["wTa"] = self.din("wTa", [NOT, 128, 32, 128], F32)
            d["qrep"] = self.din("qrep", [8, 4, 128, 2048], I32)
            d["ltbl"] = self.din("ltbl", [4, 128, 4352], F32)
            d["shv"] = self.din("shv", [128, 2], I32)
            d["plv"] = self.din("plv", [128, 8], I32)
            d["off8"] = self.din("off8", [128, 64], I32)
            d["off4"] = self.din("off4", [128, 64], I32)
            d["cw"] = self.din("cw", [128, 80, 4], F32)
            d["cb"] = self.din("cb", [128, 80], F32)
        if st != "l1":
            d["wTo"] = self.din("wTo", [16, 128, 64, 256], F32)
            d["selh"] = self.din("selh", [128, 128, 128], BF16)
            d["tri01"] = self.din("tri01", [2, 128, 256], F32)
            d["Bsel8"] = self.din("Bsel8", [8, 128, 8], BF16)
            d["g8sel"] = self.din("g8sel", [8, 8, 128], F32)
            d["drepT"] = self.din("drepT", [128, 64], F32)
            d["nw"] = self.din("nw", [128, 64], F32)
            d["fb"] = self.din("fb", [128, 4096], F32)
            d["selAe"] = self.din("selAe", [1, 1024], F32)
            d["selBf"] = self.din("selBf", [1, 8192], F32)
        d["xn_chD"] = self.dio("xn_chD", [128, 32, TH], BF16)
        d["maskD"] = self.dio("maskD", [128, 8], F32)
        d["gateD"] = self.dio("gateD", [NCH, 128, 64, 256], BF16)
        d["hsD"] = self.dio("hsD", [NCH, 128, 64, 256], BF16)
        d["bcD"] = self.dio("bcD", [128, 16, T], BF16)
        d["dtrawD"] = self.dio("dtrawD", [128, T], F32)
        d["prevD"] = self.dio("prevD", [NCH, 128, INTER], BF16)
        if st == "l1":
            d["stateD"] = self.dout("stateD", [128, 8200], BF16)
            d["wlD"] = self.dout("wlD", [512, 4096], BF16)
            d["whD"] = self.dout("whD", [512, 4096], BF16)
        elif st == "l2":
            d["stateA"] = self.din("stateA", [8, 128, 8200], BF16)
            d["wlhT"] = self.din("wlhT", [16, 128, 32, 512], BF16)
        else:
            d["stateD"] = self.nc.dram_tensor("stateD", [128, 8200], BF16).ap()
            d["wlD"] = self.nc.dram_tensor("wlD", [512, 4096], BF16).ap()
            d["whD"] = self.nc.dram_tensor("whD", [512, 4096], BF16).ap()
            d["stateA"] = self.nc.dram_tensor("stateA", [8, 128, 8200], BF16).ap()
            d["wlA"] = self.nc.dram_tensor("wlA", [4096, 4096], BF16).ap()
            d["whA"] = self.nc.dram_tensor("whA", [4096, 4096], BF16).ap()
        if st != "l1":
            import os as _os
            dk = self.dout if _os.environ.get("L2DBG") else \
                (lambda n, s, t: self.nc.dram_tensor(n, s, t).ap())
            d["ffnD"] = dk("ffnD", [8, 128, 4096], F32)
            d["gD"] = dk("gD", [NCH, 128, 64, 256], BF16)
            d["out"] = self.dout("out", [T, D], F32)
        for nm in self.dbg:
            pass
        self.dbg_written = {}

        with self.ctx:
            self.blk = self.ctx.enter_context(self.nc.Block())
            self.alloc_common()
            if st in ("l1", "fused"):
                self.ph_ln()
                if self.upto >= 2:
                    self.ph_inproj()
                    self.xn_scope.close()
                    sm, vv = self.marks["pe_inproj_done"]
                    self.pending_joins = getattr(self, "pending_joins", []) + \
                        [(sm, vv, set())]
                if self.upto >= 3:
                    self.ph_dtchain()
                if self.upto >= 4:
                    self.ph_states()
                if self.upto >= 5:
                    self.ph_dequant()
            if st == "l1" and self.upto < 2:
                self.xn_scope.close()
            if st in ("l2", "fused"):
                if st == "l2":
                    self.ph_l2_begin()
                if self.upto >= 7:
                    self.ph_exchange()
                if st == "l2":
                    self.ph_dtchain_l2()
                self.ph_bsld()
                if self.upto >= 8:
                    self.ph_ssd_y()
                if self.upto >= 9:
                    self.ph_ffn()
                if self.upto >= 10:
                    self.ph_outproj_final()
                self.mid_scope.close()
            self.ph_end()
        return self.nc

    def dbg_tap(self, name, emit_fn):
        if name in self.dbg and name not in self.dbg_written:
            self.dbg_written[name] = True
            return True
        return False

    def alloc_common(self):
        self.sident = self.sb("sident", [128, 128], BF16)
        self.sidentf = self.sb("sidentf", [128, 128], F32)
        self.sones1 = self.sb("sones1", [1, 128], BF16)
        self.sdtb = self.sb("sdtb", [128, 1], F32)
        self.salog = self.sb("salog", [128, 1], F32)
        self.mask8 = self.sb("mask8", [128, 8], F32)
        if self.stage == "l1":
            self.dt_raw = self.sb("dt_raw", [128, T], F32)
            self.dta = self.sb("dta", [128, 5, T], F32)
            self.cum_bf = self.sb("cum_bf", [128, 3, T], BF16)
            self.bsT = self.sb("bsT", [128, 8, 128], F32)
            self.ldT = self.sb("ldT", [128, 8, 128], F32)
            self.rstT = self.sb("rstT", [128, 8, 8], F32)
            self.cl = self.sb("cl", [128, 12], F32)
            self.bc_ch = self.sb("bc_ch", [128, 16, T], BF16)
            self.cb1 = self.sb("cb1", [128, 2], F32)
            self.xn_scope = ExitStack()
            self.xn_ch = self.xn_scope.enter_context(
                self.nc.sbuf_tensor("xn_ch", [128, 32, TH], BF16))
        else:
            self.cl = self.sb("cl", [128, 12], F32)
            self.cb1 = self.sb("cb1", [128, 2], F32)
            self.sdrep = self.sb("sdrep", [128, 64], BF16)
            self.snw = self.sb("snw", [128, 64], BF16)
            self.stri = self.sb("stri", [128, 2, 256], F32)
            self.sbsel = self.sb("sbsel", [128, 8, 8], BF16)
            self.g8s = self.sb("g8s", [8, 8, 128], F32)
            self.rstP = self.sb("rstP", [8, NCH, 256], F32)
            self.onesb_c = self.sb("onesb", [128, 256], F32)
            self.nae_c = self.sb("nae", [128, 1], F32)
            self.mid_scope = ExitStack()
            mse = self.mid_scope.enter_context
            self.bc_ch = mse(self.nc.sbuf_tensor("bc_ch", [128, 16, T], BF16))
            self.cum_bf = mse(self.nc.sbuf_tensor("cum_bf", [128, 3, T], BF16))
            self.sacc = mse(self.nc.sbuf_tensor("sacc", [128, INTER], BF16))
            self.bsT = mse(self.nc.sbuf_tensor("bsT", [128, 8, 128], F32))
            self.ldT = mse(self.nc.sbuf_tensor("ldT", [128, 8, 128], F32))
            self.dtsub_scope = ExitStack()
            dse = self.dtsub_scope.enter_context
            self.dt_raw = dse(self.nc.sbuf_tensor("dt_raw", [128, T], F32))
            self.dta = dse(self.nc.sbuf_tensor("dta", [128, 5, T], F32))
        self.s_ld = self.sem("s_ld")
        self.s_misc = self.sem("s_misc")
        self.s_misc2 = self.sem("s_misc2")
        self.s_end = self.sem("s_end")
        blk, d = self.blk, self.d

        @blk.gpsimd
        def _(g):
            self.s_ld.dma(g.dma_start(out=self.sident[:, :], in_=d["identb"]))
            self.s_ld.dma(g.dma_start(out=self.sidentf[:, :], in_=d["identf"]))
            self.s_ld.dma(g.dma_start(out=self.sones1[:, :], in_=d["ones1"]))
            self.s_ld.dma(g.dma_start(out=self.sdtb[:, :], in_=d["dtb"]))
            self.s_ld.dma(g.dma_start(out=self.salog[:, :], in_=d["alog"]))
        self.LD0 = self.s_ld.n

        @blk.vector
        def _(v):
            v.wait_ge(self.s_ld.h, self.LD0)
            v.tensor_scalar(self.cb1[:, 0:1], self.sdtb[:, :], 0.0, EPS, ALU.mult, ALU.add)
            self.mark(self.s_misc, v.tensor_scalar(self.cb1[:, 1:2], self.sdtb[:, :], 0.0,
                                                   1.0, ALU.mult, ALU.add), "cb1")

    def ph_ln(self):
        blk, d = self.blk, self.d
        lnscope = self.open_scope()
        xt = [self.ssb(f"xt{i}", [128, D], F32) for i in range(2)]
        xn_tok = [self.ssb(f"xntok{i}", [128, D], BF16) for i in range(2)]
        sqscr = self.ssb("sqscr", [128, D], F32)
        ln_s = self.ssb("ln_s", [128, 2, 4], F32)
        pT = self._scope.enter_context(self.nc.psum_tensor("pT", [128, 2, 1024], BF16))
        s_x = self.sem("s_x")
        s_cent = self.sem("s_cent")
        s_rstd = self.sem("s_rstd")
        s_xn = self.sem("s_xn")
        s_tr = self.sem("s_tr")
        s_te = self.sem("s_te")
        ntile = int(os.environ.get('LNT', 9))
        for t in range(ntile):
            P = 128 if t < 8 else 3
            xi, xo, ls = xt[t % 2], xn_tok[t % 2], ln_s[:, t % 2, :]

            @blk.sync
            def _(sy, t=t, P=P, xi=xi):
                if t >= 2:
                    sy.wait_ge(s_xn.h, t - 1)
                src = d["xh"][3 + 128 * t:3 + 128 * t + P, :] if t < 8 else d["xh"][0:3, :]
                s_x.dma(sy.dma_start(out=xi[0:P, :], in_=src))

            @blk.vector
            def _(v, t=t, P=P, xi=xi, ls=ls):
                v.wait_ge(s_x.h, 16 * (t + 1))
                v.tensor_reduce(ls[0:P, 0:1], xi[0:P, :], mybir.AxisListType.X, ALU.add)
                v.drain()
                v.tensor_scalar(ls[0:P, 0:1], ls[0:P, 0:1], -1.0 / D, None, ALU.mult)
                v.drain()
                s_cent.inc(v.tensor_scalar(xi[0:P, :], xi[0:P, :], ls[0:P, 0:1], None, ALU.add))

            @blk.scalar
            def _(s, t=t, P=P, xi=xi, ls=ls):
                s.wait_ge(s_cent.h, t + 1)
                if t == 0:
                    self.waitm(s, "cb1")
                s.activation(sqscr[0:P, :], xi[0:P, :], AF.Square, accum_out=ls[0:P, 1:2])
                s.drain()
                s.activation(ls[0:P, 2:3], ls[0:P, 1:2], AF.Ln, bias=self.cb1[0:P, 0:1], scale=1.0 / D)
                s.drain()
                s_rstd.inc(s.activation(ls[0:P, 2:3], ls[0:P, 2:3], AF.Exp, scale=-0.5))

            @blk.vector
            def _(v, t=t, P=P, xi=xi, xo=xo, ls=ls):
                v.wait_ge(s_rstd.h, t + 1)
                if t >= 2:
                    v.wait_ge(s_tr.h, 32 * (t - 1))
                if t < 8:
                    v.tensor_scalar(ls[0:P, 3:4], ls[0:P, 1:2], 1.0 / D, EPS, ALU.mult, ALU.add)
                    v.drain()
                    v.reciprocal(ls[0:P, 3:4], ls[0:P, 3:4])
                    v.drain()
                    v.tensor_tensor(ls[0:P, 3:4], ls[0:P, 3:4], ls[0:P, 1:2], ALU.mult)
                    v.drain()
                    v.tensor_scalar(self.mask8[0:P, t:t + 1], ls[0:P, 3:4],
                                    MASK_THRESH, None, ALU.is_gt)
                s_xn.inc(v.tensor_scalar(xo[0:P, :], xi[0:P, :], ls[0:P, 2:3], None, ALU.mult))

            @blk.tensor
            def _(te, t=t, P=P, xo=xo):
                if t == 0:
                    te.wait_ge(self.s_ld.h, self.LD0)
                te.wait_ge(s_xn.h, t + 1)
                for k in range(32):
                    i = 32 * t + k
                    if i >= 2:
                        te.wait_ge(s_te.h, i - 1)
                    s_tr.inc(te.transpose(pT[:, k % 2, 0:P], xo[0:P, 128 * k:128 * (k + 1)],
                                          self.sident[0:P, 0:P]))

            @blk.vector
            def _(v, t=t, P=P):
                col = 3 + 128 * t if t < 8 else 0
                for k in range(32):
                    v.wait_ge(s_tr.h, 32 * t + k + 1)
                    s_te.inc(v.tensor_copy(self.xn_ch[:, k, col:col + P], pT[:, k % 2, 0:P]))

        self.s_xnch = s_te
        self.s_mask = s_xn

        if os.environ.get("LNDBG"):
            dxo = self.dout("dbg_xo", [128, D], BF16)
            dls = self.dout("dbg_ls", [128, 2, 4], F32)

            @blk.sync
            def _(sy):
                sy.wait_ge(s_te.h, s_te.n)
                self.s_end.dma(sy.dma_start(out=dxo, in_=xn_tok[0][:, :]))
                self.s_end.dma(sy.dma_start(out=dls, in_=ln_s[:, :, :]))

        @blk.sync
        def _(sy):
            sy.wait_ge(s_te.h, s_te.n)
            sy.wait_ge(s_xn.h, s_xn.n)
            self.s_end.dma(sy.dma_start(out=d["xn_chD"], in_=self.xn_ch[:, :, :]))
            self.s_end.dma(sy.dma_start(out=d["maskD"], in_=self.mask8[:, :]))
        self.close_scope(joins=[(s_te, s_te.n), (s_tr, s_tr.n), (s_xn, s_xn.n)])

    def ph_inproj(self):
        blk, d = self.blk, self.d
        NW = 3
        ipscope = self.open_scope()
        wA = [self.ssb(f"wA{i}", [128, 32, 128], BF16) for i in range(NW)]
        xbc = [self.ssb(f"xbc{i}", [128, TH], F32) for i in range(2)]
        cva = [self.ssb(f"cva{i}", [128, T], F32) for i in range(2)]
        evb = [self.ssb(f"evb{i}", [128, T], BF16) for i in range(2)]
        scw = self.ssb("scw", [128, 80, 4], F32)
        scb = self.ssb("scb", [128, 80], F32)
        pA = self._scope.enter_context(self.nc.psum_tensor("pA", [128, 4, 512], F32))
        pHa = self._scope.enter_context(self.nc.psum_tensor("pHa", [128, 2, 512], F32))
        s_w = self.sem("s_w")
        s_mm = self.sem("s_mm")
        s_ev = self.sem("s_ev")
        s_cv = self.sem("s_cv")
        s_rt = self.sem("s_rt")
        s_cacc = self.sem("s_cacc")
        self.s_convdone = s_cv

        @blk.gpsimd
        def _(g):
            self.apply_joins(g, "ip0")
            self.s_ld.dma(g.dma_start(out=scw[:, :, :], in_=d["cw"]))
            self.s_ld.dma(g.dma_start(out=scb[:, :], in_=d["cb"]))

        has_evb = [ot != 0 for ot in range(NOT)]
        cnt_cv = np.cumsum(has_evb).tolist()

        NOTL = int(os.environ.get('IPT', NOT))
        for ot in range(NOTL):
            isdt, isg = ot == 0, ot >= 81
            isb, isc = 1 <= ot <= 8, 9 <= ot <= 16
            isxbc = (not isdt) and (not isg)
            wbuf = wA[ot % NW]
            par = ot % 2

            @blk.gpsimd
            def _(g, ot=ot, wbuf=wbuf):
                self.apply_joins(g, "ip0")
                if ot >= NW:
                    g.wait_ge(s_mm.h, ot - NW + 1)
                s_w.dma(g.dma_start(out=wbuf[:, :, :], in_=d["wTa"][ot, :, :, :]))

            @blk.tensor
            def _(te, ot=ot, wbuf=wbuf, par=par, isxbc=isxbc):
                import os
                self.apply_joins(te, "ip0")
                te.wait_ge(s_w.h, 16 * (ot + 1))
                if ot == 0:
                    te.wait_ge(self.s_xnch.h, self.s_xnch.n)
                if ot >= 2:
                    te.wait_ge(s_ev.h, ot - 1)
                last = None
                for c in range(2):
                    for k in range(32):
                        last = te.matmul(pA[:, 2 * par + c, :], wbuf[:, k, :],
                                         self.xn_ch[:, k, 3 + 512 * c:3 + 512 * (c + 1)],
                                         start=(k == 0), stop=(k == 31))
                if isxbc:
                    for k in range(32):
                        last = te.matmul(pHa[:, par, 0:3], wbuf[:, k, :],
                                         self.xn_ch[:, k, 0:3],
                                         start=(k == 0), stop=(k == 31))
                s_mm.inc(last)

            if isdt:
                @blk.scalar
                def _(s, ot=ot, par=par):
                    self.apply_joins(s, "ip0")
                    s.wait_ge(s_mm.h, ot + 1)
                    s.copy(self.dt_raw[:, 0:512], pA[:, 2 * par + 0, :])
                    s.copy(self.dt_raw[:, 512:1024], pA[:, 2 * par + 1, :])
                    self.mark(s_ev, s.drain(), "dtraw")

                @blk.sync
                def _(sy):
                    sy.wait_ge(s_ev.h, 1)
                    self.s_end.dma(sy.dma_start(out=d["dtrawD"], in_=self.dt_raw[:, :]))
            elif isg:
                gi = ot - 81
                eb = evb[ot % 2]

                @blk.scalar
                def _(s, ot=ot, par=par, eb=eb):
                    self.apply_joins(s, "ip0")
                    s.wait_ge(s_mm.h, ot + 1)
                    if ot >= 2:
                        self.waitm(s, f"rt{ot - 2}")
                    s.activation(eb[:, 0:512], pA[:, 2 * par + 0, :], AF.Silu)
                    s_ev.inc(s.activation(eb[:, 512:1024], pA[:, 2 * par + 1, :], AF.Silu))
                    s_cv.inc(s.drain())

                @blk.sync
                def _(sy, ot=ot, gi=gi, eb=eb):
                    self.apply_joins(sy, "ip0")
                    sy.wait_ge(s_cv.h, cnt_cv[ot])
                    self.mark(s_rt, sy.dma_start(
                        out=d["gateD"][:, :, gi, :].rearrange("c p t -> p c t"),
                        in_=eb.rearrange("p (c t) -> p c t", c=NCH)), f"rt{ot}", 16)
            else:
                xb, cv, eb = xbc[ot % 2], cva[ot % 2], evb[ot % 2]
                if isb:
                    ct = 64 + (ot - 1)
                elif isc:
                    ct = 72 + (ot - 9)
                else:
                    ct = ot - 17

                @blk.scalar
                def _(s, ot=ot, par=par, xb=xb):
                    self.apply_joins(s, "ip0")
                    s.wait_ge(s_mm.h, ot + 1)
                    if ot >= 3:
                        s.wait_ge(s_cv.h, cnt_cv[ot - 2])
                    s.copy(xb[:, 0:3], pHa[:, par, 0:3])
                    s.copy(xb[:, 3:515], pA[:, 2 * par + 0, :])
                    s.copy(xb[:, 515:1027], pA[:, 2 * par + 1, :])
                    s_ev.inc(s.drain())

                @blk.vector
                def _(v, ot=ot, xb=xb, cv=cv, ct=ct):
                    self.apply_joins(v, "ip0")
                    v.wait_ge(s_ev.h, ot + 1)
                    if ot >= 3:
                        v.wait_ge(s_cv.h, cnt_cv[ot - 2])
                    v.tensor_scalar(cv[:, :], xb[:, 0:T], scw[:, ct, 0:1], scb[:, ct:ct + 1],
                                    ALU.mult, ALU.add)
                    last = None
                    for k in (1, 2, 3):
                        last = v.scalar_tensor_tensor(cv[:, :], xb[:, k:k + T],
                                                      scw[:, ct, k:k + 1],
                                                      cv[:, :], ALU.mult, ALU.add)
                    s_cacc.inc(last)

                @blk.scalar
                def _(s, ot=ot, cv=cv, eb=eb):
                    s.wait_ge(s_cacc.h, sum(1 for j in range(1, ot + 1) if j <= 80))
                    if ot >= 2 and has_evb[ot - 2]:
                        self.waitm(s, f"rt{ot - 2}")
                    s_cv.inc(s.activation(eb[:, :], cv[:, :], AF.Silu))

                if isb or isc:
                    bci = ot - 1

                    @blk.vector
                    def _(v, ot=ot, eb=eb, bci=bci):
                        v.wait_ge(s_cv.h, cnt_cv[ot])
                        self.mark(self.s_misc, v.tensor_copy(self.bc_ch[:, bci, :],
                                                             eb[:, :]), f"rt{ot}")
                else:
                    hi = ot - 17

                    @blk.sync
                    def _(sy, ot=ot, eb=eb, hi=hi):
                        sy.wait_ge(s_cv.h, cnt_cv[ot])
                        self.mark(s_rt, sy.dma_start(
                            out=d["hsD"][:, :, hi, :].rearrange("c p t -> p c t"),
                            in_=eb.rearrange("p (c t) -> p c t", c=NCH)), f"rt{ot}", 16)

        self.s_inproj_rt = s_rt
        self.s_inproj_mm = s_mm
        self.s_misc_inproj = self.s_misc.n

        s_ped = self.sem("s_ped")

        @blk.tensor
        def _(te):
            self.mark(s_ped, te.drain(), "pe_inproj_done")

        @blk.sync
        def _(sy):
            sy.wait_ge(s_rt.h, s_rt.n)
            sy.wait_ge(self.s_misc.h, self.s_misc.n)
            self.s_end.dma(sy.dma_start(out=d["bcD"], in_=self.bc_ch[:, :, :]))
        self.close_scope(joins=[(s_rt, s_rt.n), (self.s_misc, self.s_misc.n),
                                (s_ev, s_ev.n), (s_cv, s_cv.n),
                                (s_mm, s_mm.n), (s_cacc, s_cacc.n)])


    def open_scope(self):
        self._scope = ExitStack()
        return self._scope

    def close_scope(self, joins):
        self._scope.close()
        self.pending_joins = getattr(self, "pending_joins", []) + \
            [(s, v, set()) for (s, v) in joins]

    def apply_joins(self, eng, tag):
        for (s, v, done) in getattr(self, "pending_joins", []):
            key = (tag, id(eng))
            if key not in done:
                done.add(key)
                eng.wait_ge(s.h, v)

    def ssb(self, name, shape, dt):
        return self._scope.enter_context(
            self.nc.sbuf_tensor(name, shape, dt, unsafe_allow_realloc=True))

    def ph_dtchain(self, from_dram=False):
        blk, d = self.blk, self.d
        dta, cl = self.dta, self.cl
        s_dt = self.sem("s_dt")
        self.s_dtchain = s_dt
        onesb = getattr(self, "onesb_c", None)
        nae = getattr(self, "nae_c", None)
        if onesb is None:
            onesb = self.sb("onesb", [128, 256], F32)
            nae = self.sb("nae", [128, 1], F32)

        s_dtld = self.sem("s_dtld")
        if from_dram:
            @blk.sync
            def _(sy):
                s_dtld.dma(sy.dma_start(out=self.dt_raw[:, :], in_=d["dtrawD"]))

        @blk.vector
        def _(v):
            self.apply_joins(v, "dtc")
            if from_dram:
                v.wait_ge(s_dtld.h, 16)
            else:
                self.waitm(v, "dtraw")
            v.wait_ge(self.s_ld.h, self.LD0)
            v.tensor_scalar(dta[:, 2, :], self.dt_raw[:, :], self.sdtb[:, :], None, ALU.add)
            s_dt.inc(v.drain())

        @blk.scalar
        def _(s):
            self.apply_joins(s, "dtc")
            s.wait_ge(s_dt.h, s_dt.n)
            s.activation(dta[:, 1, :], dta[:, 2, :], AF.Abs)
            s.drain()
            s.activation(dta[:, 1, :], dta[:, 1, :], AF.Exp, scale=-1.0)
            s.drain()
            s_dt.inc(s.activation(dta[:, 1, :], dta[:, 1, :], AF.Ln, bias=self.cb1[:, 1:2]))

        @blk.vector
        def _(v):
            v.wait_ge(s_dt.h, s_dt.n)
            v.tensor_scalar(dta[:, 0, :], dta[:, 2, :], 0.0, None, ALU.max)
            v.drain()
            v.tensor_tensor(dta[:, 0, :], dta[:, 0, :], dta[:, 1, :], ALU.add)
            v.drain()
            s_dt.inc(v.tensor_scalar(dta[:, 0, :], dta[:, 0, :], 1.1754944e-38, None, ALU.max))

        @blk.scalar
        def _(s):
            s.wait_ge(s_dt.h, s_dt.n)
            s.activation(dta[:, 1, :], dta[:, 0, :], AF.Ln)
            s_dt.inc(s.activation(nae[:, :], self.salog[:, :], AF.Exp, scale=1.0))

        @blk.vector
        def _(v):
            v.wait_ge(s_dt.h, s_dt.n)
            v.tensor_scalar(nae[:, :], nae[:, :], -1.0, None, ALU.mult)
            v.drain()
            v.tensor_scalar(dta[:, 2, :], dta[:, 0, :], nae[:, :], None, ALU.mult)
            v.tensor_scalar(onesb[:, :], self.dt_raw[:, 0:256], 0.0, 1.0, ALU.mult, ALU.add)
            v.drain()
            for c in range(NCH):
                v.tensor_tensor_scan(dta[:, 3, 256 * c:256 * (c + 1)], onesb[:, :],
                                     dta[:, 2, 256 * c:256 * (c + 1)], 0.0, ALU.mult, ALU.add)
            v.drain()
            v.scalar_tensor_tensor(dta[:, 4, :], dta[:, 3, :], -1.0, dta[:, 1, :],
                                   ALU.mult, ALU.add)
            v.tensor_copy(self.cum_bf[:, 0, :], dta[:, 3, :])
            v.drain()
            v.tensor_tensor(dta[:, 2, :], dta[:, 3, :], self.cum_bf[:, 0, :], ALU.subtract)
            v.drain()
            v.tensor_copy(self.cum_bf[:, 1, :], dta[:, 2, :])
            v.drain()
            v.tensor_tensor(dta[:, 2, :], dta[:, 2, :], self.cum_bf[:, 1, :], ALU.subtract)
            v.drain()
            v.tensor_copy(self.cum_bf[:, 2, :], dta[:, 2, :])
            for c in range(NCH):
                v.tensor_copy(cl[:, c:c + 1], dta[:, 3, 256 * c + 255:256 * c + 256])
            s_dt.inc(v.drain())

        @blk.scalar
        def _(s):
            s.wait_ge(s_dt.h, s_dt.n)
            for c in range(NCH):
                s.activation(cl[:, 4 + c:5 + c], cl[:, c:c + 1], AF.Exp)
            s_dt.inc(s.drain())

        @blk.vector
        def _(v):
            v.wait_ge(s_dt.h, s_dt.n)
            v.tensor_scalar(cl[:, 8:9], cl[:, 0:1], 0.0, 1.0, ALU.mult, ALU.add)
            v.tensor_copy(cl[:, 9:10], cl[:, 4:5])
            v.drain()
            v.tensor_tensor(cl[:, 10:11], cl[:, 9:10], cl[:, 5:6], ALU.mult)
            v.drain()
            v.tensor_tensor(cl[:, 11:12], cl[:, 10:11], cl[:, 6:7], ALU.mult)
            s_dt.inc(v.drain())
        self.s_dtdone = s_dt

    def mark(self, sem, inst, key, k=1):
        sem.inc(inst, k)
        if not hasattr(self, "marks"):
            self.marks = {}
        self.marks[key] = (sem, sem.n)

    def waitm(self, eng, key):
        s, v = self.marks[key]
        eng.wait_ge(s.h, v)

    def ph_states(self):
        blk, d = self.blk, self.d
        sc = self.open_scope()
        hs_cc = self.ssb("hs_cc", [128, 64, 256], BF16)
        xcT = self.ssb("xcT", [128, 2, INTER], BF16)
        bTt = self.ssb("bTt", [128, 2, 8, 128], BF16)
        ddT = self.ssb("ddT", [128, 2, 128], F32)
        dtl = self.ssb("dtl", [128, 256], F32)
        xsg = [self.ssb(f"xsg{i}", [128, 2, 1024], BF16) for i in range(2)]
        srun = self.ssb("srun", [128, INTER], BF16)
        cdb = self.ssb("cdb", [128, 128], F32)
        prow = self.ssb("prow", [1, 128], BF16)
        prodv = self.ssb("prodv", [128, 1], BF16)
        psc = self._scope.enter_context
        pS = psc(self.nc.psum_tensor("pS", [128, 1024], F32))
        pX = psc(self.nc.psum_tensor("pX", [128, 2, 1024], BF16))
        pXf = psc(self.nc.psum_tensor("pXf", [128, 2, 512], F32))
        pC = psc(self.nc.psum_tensor("pC", [1, 512], F32))
        pB = psc(self.nc.psum_tensor("pB", [128, 512], F32))
        s_h = self.sem("s_h")
        s_t = self.sem("s_t")
        s_tv = self.sem("s_tv")
        s_ss = self.sem("s_ss")
        s_up = self.sem("s_up")
        s_pv = self.sem("s_pv")
        s_mka = self.sem("s_mka")
        s_mkp = self.sem("s_mkp")
        s_mkv = self.sem("s_mkv")

        @blk.sync
        def _(sy):
            self.apply_joins(sy, "st0")
            self.mark(s_pv, sy.dma_start(out=srun[:, :], in_=self.d["zer"]), "srun_init", 16)

        NCHL = int(os.environ.get('STNC', NCH))
        STSER = os.environ.get('STSER', '')
        for c in range(NCHL):
            @blk.sync
            def _(sy, c=c):
                self.apply_joins(sy, "st0")
                if c > 0 and "h" in STSER:
                    self.waitm(sy, f"scan{c - 1}")
                if c > 0:
                    self.waitm(sy, f"hs_done{c - 1}")
                sy.wait_ge(self.s_inproj_rt.h, self.s_inproj_rt.n)
                sy.wait_ge(self.s_misc.h, self.s_misc_inproj)
                self.mark(s_h, sy.dma_start(out=hs_cc[:, :, :], in_=d["hsD"][c, :, :, :]),
                          f"hs{c}", 16)

            @blk.tensor
            def _(te, c=c):
                self.apply_joins(te, "st0")
                if c > 0 and "t" in STSER:
                    self.waitm(te, f"scan{c - 1}")
                self.waitm(te, f"hs{c}")
                te.wait_ge(self.s_dtdone.h, self.s_dtdone.n)
                n = 0
                for m in range(64):
                    for half in range(2):
                        if s_t.n >= 2:
                            te.wait_ge(s_tv.h, s_t.n - 1)
                        inst = te.transpose(pX[:, n % 2, 0:128],
                                            hs_cc[:, m, 128 * half:128 * (half + 1)],
                                            self.sident[:, :])
                        if m == 63 and half == 1:
                            self.mark(s_t, inst, f"hs_done{c}")
                        else:
                            s_t.inc(inst)
                        n += 1
                for g in range(8):
                    for half in range(2):
                        te.wait_ge(s_tv.h, s_t.n - 1)
                        s_t.inc(te.transpose(pX[:, n % 2, 0:128],
                                             self.bc_ch[:, g, 256 * c + 128 * half:
                                                         256 * c + 128 * (half + 1)],
                                             self.sident[:, :]))
                        n += 1

            @blk.vector
            def _(v, c=c):
                self.apply_joins(v, "st0")
                base = s_t.n - 144
                n = 0
                for m in range(64):
                    for half in range(2):
                        v.wait_ge(s_t.h, base + n + 1)
                        s_tv.inc(v.tensor_copy(xcT[:, half, 128 * m:128 * (m + 1)],
                                               pX[:, n % 2, 0:128]))
                        n += 1
                for g in range(8):
                    for half in range(2):
                        v.wait_ge(s_t.h, base + n + 1)
                        s_tv.inc(v.tensor_copy(bTt[:, half, g, :], pX[:, n % 2, 0:128]))
                        n += 1
                self.marks[f"tev{c}"] = (s_tv, s_tv.n)

            @blk.scalar
            def _(s, c=c):
                self.apply_joins(s, "st0")
                if c > 0 and "d" in STSER:
                    self.waitm(s, f"scan{c - 1}")
                s.wait_ge(self.s_dtdone.h, self.s_dtdone.n)
                if c > 0:
                    self.waitm(s, f"ddT{c - 1}")
                self.mark(s_mka, s.activation(dtl[:, :], self.dta[:, 4, 256 * c:256 * (c + 1)],
                                              AF.Exp, bias=self.cl[:, c:c + 1]), f"dtl{c}")

            @blk.tensor
            def _(te, c=c):
                self.waitm(te, f"dtl{c}")
                if c > 0:
                    self.waitm(te, f"cdb{c - 1}")
                te.transpose(pXf[:, 0, 0:128], dtl[:, 0:128], self.sidentf[:, :])
                te.transpose(pXf[:, 1, 0:128], dtl[:, 128:256], self.sidentf[:, :])
                self.mark(s_mkp, te.transpose(pC[:, 0:128], self.cl[:, 4 + c:5 + c],
                                             self.sidentf[:, :]), f"ddT{c}")

            @blk.vector
            def _(v, c=c):
                self.waitm(v, f"ddT{c}")
                v.tensor_copy(ddT[:, 0, :], pXf[:, 0, 0:128])
                v.tensor_copy(ddT[:, 1, :], pXf[:, 1, 0:128])
                self.mark(s_mkv, v.tensor_copy(prow[:, :], pC[:, 0:128]), f"prow{c}")

            @blk.tensor
            def _(te, c=c):
                self.waitm(te, f"prow{c}")
                self.mark(s_mkp, te.matmul(pB[:, 0:128], self.sones1[:, :], prow[:, :],
                                          start=True, stop=True), f"pB{c}")

            @blk.vector
            def _(v, c=c):
                self.waitm(v, f"pB{c}")
                v.tensor_copy(cdb[:, :], pB[:, 0:128])
                self.mark(s_mkv, v.drain(), f"cdb{c}")

            @blk.gpsimd
            def _(g_, c=c):
                if c == 0:
                    self.waitm(g_, "srun_init")
                else:
                    self.waitm(g_, f"scan{c - 1}")
                self.mark(s_pv, g_.dma_start(out=d["prevD"][c, :, :], in_=srun[:, :]),
                          f"prev{c}", 16)

            for g in range(8):
                xg = xsg[g % 2]

                @blk.vector
                def _(v, c=c, g=g, xg=xg):
                    self.waitm(v, f"cdb{c}")
                    self.waitm(v, f"tev{c}")
                    if c * 8 + g >= 2:
                        v.wait_ge(s_ss.h, c * 8 + g - 1)
                    for half in range(2):
                        dd = ddT[:, half, 16 * g:16 * (g + 1)].broadcast_to([128, 16, 64])
                        v.tensor_tensor(xg[:, half, :].rearrange("p (h q) -> p h q", h=16),
                                        xcT[:, half, 1024 * g:1024 * (g + 1)].rearrange(
                                            "p (h q) -> p h q", h=16),
                                        dd, ALU.mult)
                    self.mark(s_mkv, v.drain(), f"xg{c}_{g}")

                @blk.tensor
                def _(te, c=c, g=g, xg=xg):
                    self.waitm(te, f"xg{c}_{g}")
                    if g == 0:
                        self.waitm(te, f"prev{c}")
                        if c > 0:
                            self.waitm(te, f"scan{c - 1}")
                    else:
                        self.waitm(te, f"scang{c}_{g - 1}")
                    last = None
                    for hh in range(16):
                        for half in range(2):
                            last = te.matmul(pS[:, 64 * hh:64 * (hh + 1)],
                                             bTt[:, half, g, :],
                                             xg[:, half, 64 * hh:64 * (hh + 1)],
                                             start=(half == 0), stop=(half == 1))
                    s_ss.inc(last)

                @blk.vector
                def _(v, c=c, g=g):
                    v.wait_ge(s_ss.h, s_ss.n)
                    self.waitm(v, f"prev{c}")
                    inst = None
                    for hh in range(16):
                        h = 16 * g + hh
                        inst = v.scalar_tensor_tensor(srun[:, 64 * h:64 * (h + 1)],
                                                      srun[:, 64 * h:64 * (h + 1)],
                                                      cdb[:, h:h + 1],
                                                      pS[:, 64 * hh:64 * (hh + 1)],
                                                      ALU.mult, ALU.add)
                    self.mark(s_up, inst, f"scang{c}_{g}")
                    if g == 7:
                        self.mark(s_up, v.drain(), f"scan{c}")

        if os.environ.get("STDBG"):
            dd1 = self.dout("dbg_dtl", [128, 256], F32)
            dd2 = self.dout("dbg_ddT", [128, 2, 128], F32)
            dd3 = self.dout("dbg_xcT", [128, 2, INTER], BF16)
            dd4 = self.dout("dbg_bTt", [128, 2, 8, 128], BF16)
            dd5 = self.dout("dbg_xsg", [128, 2, 1024], BF16)
            dd6 = self.dout("dbg_srun", [128, INTER], BF16)
            dd7 = self.dout("dbg_cdb", [128, 128], F32)

            @blk.gpsimd
            def _(g):
                self.waitm(g, "prev1")
                self.s_end.dma(g.dma_start(out=dd1, in_=dtl[:, :]))
                self.s_end.dma(g.dma_start(out=dd2, in_=ddT[:, :, :]))
                self.s_end.dma(g.dma_start(out=dd3, in_=xcT[:, :, :]))
                self.s_end.dma(g.dma_start(out=dd4, in_=bTt[:, :, :, :]))
                self.s_end.dma(g.dma_start(out=dd5, in_=xsg[0][:, :, :]))
                self.s_end.dma(g.dma_start(out=dd6, in_=srun[:, :]))
                self.s_end.dma(g.dma_start(out=dd7, in_=cdb[:, :]))
                g.wait_ge(self.s_end.h, self.s_end.n)

        @blk.vector
        def _(v):
            self.waitm(v, f"scan{NCHL - 1}")
            v.tensor_tensor(prodv[:, :], self.cl[:, 4:5], self.cl[:, 5:6], ALU.mult)
            v.drain()
            v.tensor_tensor(prodv[:, :], prodv[:, :], self.cl[:, 6:7], ALU.mult)
            v.drain()
            self.mark(s_up, v.tensor_tensor(prodv[:, :], prodv[:, :], self.cl[:, 7:8],
                                            ALU.mult), "prodv")

        @blk.gpsimd
        def _(g):
            self.waitm(g, f"scan{NCHL - 1}")
            self.waitm(g, "prodv")
            self.s_end.dma(g.dma_start(out=d["stateD"][:, 0:8192], in_=srun[:, :]))
            self.s_end.dma(g.dma_start(out=d["stateD"][:, 8192:8193], in_=prodv[:, :],
                                       allow_slow_non_contiguous=True))
        self.close_scope(joins=[(self.s_end, self.s_end.n), (s_up, s_up.n),
                                (s_ss, s_ss.n), (s_pv, s_pv.n), (s_t, s_t.n),
                                (s_tv, s_tv.n), (s_mka, s_mka.n), (s_mkp, s_mkp.n),
                                (s_mkv, s_mkv.n)])

    def ph_dequant(self):
        blk, d = self.blk, self.d
        sc = self.open_scope()
        sq = self.ssb("sq", [128, 64, 32], I32)
        dqs = self.ssb("dqs", [128, 4096], F32)
        acc = self.ssb("acc", [128, 64, 64], I32)
        bba = self.ssb("bba", [128, 64, 32], I32)
        u8 = self.ssb("u8", [128, 64, 64], U16)
        u4 = self.ssb("u4", [128, 64, 64], U16)
        soff8 = self.ssb("soff8", [128, 64], I32)
        soff4 = self.ssb("soff4", [128, 64], I32)
        sshv = self.ssb("sshv", [128, 2], I32)
        splv = self.ssb("splv", [128, 8], I32)
        stbl = self.ssb("stbl", [128, 4352], F32)
        gout = self.ssb("gout", [128, 1024, 1], F32)
        gst = self.ssb("gst", [128, 4096], F32)
        s_q = self.sem("s_q")
        s_i = self.sem("s_i")
        s_g = self.sem("s_g")
        s_o = self.sem("s_o")

        @blk.gpsimd
        def _(g):
            self.apply_joins(g, "dq0")
            s_q.dma(g.dma_start(out=sshv[:, :], in_=d["shv"]))
            s_q.dma(g.dma_start(out=splv[:, :], in_=d["plv"]))
            s_q.dma(g.dma_start(out=soff8[:, :], in_=d["off8"]))
            s_q.dma(g.dma_start(out=soff4[:, :], in_=d["off4"]))
        QBASE = s_q.n

        for tile in range(4):
            @blk.gpsimd
            def _(g, tile=tile):
                if tile > 0:
                    self.waitm(g, f"idx{tile - 1}")
                self.mark(s_q, g.dma_start(out=stbl[:, :], in_=d["ltbl"][tile, :, :]),
                          f"tbl{tile}", 16)

            for pl in range(8):
                @blk.gpsimd
                def _(g, tile=tile, pl=pl):
                    if pl > 0 or tile > 0:
                        self.waitm(g, f"bit{tile}_{pl - 1}" if pl > 0 else f"idx{tile - 1}")
                    self.mark(s_q, g.dma_start(
                        out=sq[:, :, :],
                        in_=d["qrep"][pl, tile, :, :].rearrange("p (a b) -> p a b", a=64)),
                        f"q{tile}_{pl}", 16)

                @blk.vector
                def _(v, tile=tile, pl=pl):
                    self.apply_joins(v, "dq0")
                    self.waitm(v, f"q{tile}_{pl}")
                    v.wait_ge(s_q.h, QBASE)
                    if tile > 0 and pl == 0:
                        self.waitm(v, f"gath{tile - 1}_1")
                    inst = None
                    for pi in range(2):
                        v.tensor_scalar(bba[:, :, :], sq[:, :, :], sshv[:, pi:pi + 1], 1,
                                        ALU.logical_shift_right, ALU.bitwise_and)
                        dst = acc[:, :, pi::2]
                        if pl == 0:
                            inst = v.tensor_copy(dst, bba[:, :, :])
                        else:
                            inst = v.scalar_tensor_tensor(dst, bba[:, :, :],
                                                          splv[:, pl:pl + 1], dst,
                                                          ALU.logical_shift_left,
                                                          ALU.bitwise_or)
                    self.mark(s_i, inst, f"bit{tile}_{pl}")

            @blk.vector
            def _(v, tile=tile):
                self.waitm(v, f"bit{tile}_7")
                o8 = soff8[:, :].unsqueeze(-1).broadcast_to([128, 64, 64])
                o4 = soff4[:, :].unsqueeze(-1).broadcast_to([128, 64, 64])
                v.tensor_tensor(u8[:, :, :], acc[:, :, :], o8, ALU.add)
                t4 = dqs[:, :].bitcast(I32).rearrange("p (a b) -> p a b", a=64)
                v.tensor_scalar(t4, acc[:, :, :], 15, None, ALU.bitwise_and)
                inst = v.tensor_tensor(u4[:, :, :], t4, o4, ALU.add)
                self.mark(s_i, inst, f"idx{tile}")

            for li, (uu, wD, offname) in enumerate(((u8, "whD", "off8"), (u4, "wlD", "off4"))):
                for r16 in range(16):
                    @blk.gpsimd
                    def _(g, tile=tile, li=li, uu=uu, r16=r16):
                        self.waitm(g, f"idx{tile}")
                        self.waitm(g, f"tbl{tile}")
                        if (tile, li, r16) != (0, 0, 0):
                            self.waitm(g, "lastspill")
                        last = None
                        for q in range(4):
                            g.indirect_copy(gout[:, :, :], stbl[:, :], uu[:, 4 * r16 + q, :],
                                            i_know_ap_gather_is_preferred=True)
                            last = g.tensor_copy(gst[:, 1024 * q:1024 * (q + 1)],
                                                 gout[:, :, 0])
                        self.mark(s_g, last, f"gath{tile}_{li}" if r16 == 15
                                  else f"g{tile}_{li}_{r16}")

                    @blk.gpsimd
                    def _(g, tile=tile, li=li, wD=wD, r16=r16):
                        self.waitm(g, f"g{tile}_{li}_{r16}" if r16 < 15 else f"gath{tile}_{li}")
                        self.mark(s_o, g.dma_start(
                            out=d[wD][128 * tile + r16:128 * tile + r16 + 113:16, :],
                            in_=gst[r16:r16 + 113:16, :]), "lastspill", 16)
        self.close_scope(joins=[(s_o, s_o.n), (s_g, s_g.n), (s_i, s_i.n)])

    def ph_end(self):
        blk = self.blk

        @blk.sync
        def _(sy):
            self.apply_joins(sy, "end")
            sy.wait_ge(self.s_end.h, self.s_end.n)

    def ph_dtchain_l2(self):
        self.ph_dtchain(from_dram=True)

    def ph_l2_begin(self):
        blk, d = self.blk, self.d
        s_l2 = self.sem("s_l2")
        self.s_l2 = s_l2

        @blk.sync
        def _(sy):
            s_l2.dma(sy.dma_start(out=self.mask8[:, :], in_=d["maskD"]))
            s_l2.dma(sy.dma_start(out=self.bc_ch[:, :, :], in_=d["bcD"]))
            s_l2.dma(sy.dma_start(
                out=self.stri[:, :, :], in_=d["tri01"].rearrange("a p i -> p a i")))
            s_l2.dma(sy.dma_start(
                out=self.sbsel[:, :, :], in_=d["Bsel8"].rearrange("g p i -> p g i")))
            s_l2.dma(sy.dma_start(out=self.g8s[:, :, :],
                                  in_=d["g8sel"].rearrange("g k i -> k g i")))

        self.L2LD = s_l2.n
        s_l2g = self.sem("s_l2g")
        self.s_l2g = s_l2g

        @blk.gpsimd
        def _(g):
            s_l2g.dma(g.dma_start(out=self.sdrep[:, :], in_=d["drepT"]))
            s_l2g.dma(g.dma_start(out=self.snw[:, :], in_=d["nw"]))

    def ph_exchange(self):
        blk, d = self.blk, self.d
        sc = self.open_scope()
        sm = [self.ssb(f"sm{i}", [128, INTER], BF16) for i in range(2)]
        t2 = self.ssb("t2", [128, INTER], BF16)
        prowp = self.ssb("prowp", [1, 1024], F32)
        selA_s = self.ssb("selA_s", [1, 1024], F32)
        selB_s = self.ssb("selB_s", [1, 8192], F32)
        coef = self.ssb("coef", [1, 1024], F32)
        tsc = self.ssb("tsc", [1, 1024], F32)
        coefb = self.ssb("coefb", [1, 1024], BF16)
        cfb = self.ssb("cfb", [128, 1024], F32)
        pcf = self._scope.enter_context(self.nc.psum_tensor("pcf", [128, 1024], F32))
        s_xp = self.sem("s_xp")
        s_xg = self.sem("s_xg")
        s_xv = self.sem("s_xv")
        s_xt = self.sem("s_xt")

        @blk.gpsimd
        def _(g):
            self.apply_joins(g, "ex0")
            for m in range(8):
                s_xg.dma(g.dma_start(
                    out=prowp[0:1, 128 * m:128 * (m + 1)],
                    in_=d["stateA"][m, :, 8192:8193].rearrange("h u -> u h"),
                    allow_slow_non_contiguous=True))

        @blk.sync
        def _(sy):
            self.apply_joins(sy, "ex0")
            s_xp.dma(sy.dma_start(out=selA_s[:, :], in_=d["selAe"]))
            s_xp.dma(sy.dma_start(out=selB_s[:, :], in_=d["selBf"]))

        @blk.vector
        def _(v):
            self.apply_joins(v, "ex0")
            v.wait_ge(s_xp.h, s_xp.n)
            v.wait_ge(s_xg.h, s_xg.n)
            v.tensor_copy(coef[:, :], selA_s[:, :])
            for j in range(8):
                pj = prowp[0:1, 128 * j:128 * (j + 1)].rearrange(
                    "u (m h) -> u m h", m=1).broadcast_to([1, 8, 128])
                v.drain()
                v.tensor_scalar(tsc[:, :].rearrange("one (m h) -> one m h", m=8),
                                pj, -1.0, None, ALU.add)
                v.drain()
                v.tensor_tensor(tsc[:, :], tsc[:, :],
                                selB_s[0:1, 1024 * j:1024 * (j + 1)], ALU.mult)
                v.drain()
                v.tensor_scalar(tsc[:, :], tsc[:, :], 1.0, None, ALU.add)
                v.drain()
                v.tensor_tensor(coef[:, :], coef[:, :], tsc[:, :], ALU.mult)
            v.drain()
            self.mark(s_xv, v.tensor_copy(coefb[:, :], coef[:, :]), "coefb")

        @blk.tensor
        def _(te):
            self.apply_joins(te, "ex0")
            self.waitm(te, "coefb")
            te.wait_ge(self.s_ld.h, self.LD0)
            te.matmul(pcf[:, 0:512], self.sones1[:, :], coefb[0:1, 0:512],
                      start=True, stop=True)
            self.mark(s_xt, te.matmul(pcf[:, 512:1024], self.sones1[:, :],
                                      coefb[0:1, 512:1024], start=True, stop=True), "pcf")

        @blk.vector
        def _(v):
            self.waitm(v, "pcf")
            self.mark(s_xv, v.tensor_copy(cfb[:, :], pcf[:, :]), "cfb")

        for m in range(8):
            @blk.sync
            def _(sy, m=m):
                if m >= 2:
                    self.waitm(sy, f"sacc{m - 2}")
                s_xp.dma(sy.dma_start(out=sm[m % 2][:, :], in_=d["stateA"][m, :, 0:8192]))

            @blk.vector
            def _(v, m=m):
                v.wait_ge(s_xp.h, 32 + 16 * (m + 1))
                self.waitm(v, "cfb")
                cf = cfb[:, 128 * m:128 * (m + 1)].rearrange(
                    "p (h q) -> p h q", q=1).broadcast_to([128, 128, 64])
                smv = sm[m % 2][:, :].rearrange("p (h q) -> p h q", h=128)
                if m == 0:
                    v.tensor_tensor(self.sacc[:, :].rearrange(
                        "p (h q) -> p h q", h=128), smv, cf, ALU.mult)
                else:
                    v.tensor_tensor(t2[:, :].rearrange("p (h q) -> p h q", h=128),
                                    smv, cf, ALU.mult)
                    v.drain()
                    v.tensor_tensor(self.sacc[:, :], self.sacc[:, :], t2[:, :],
                                    ALU.add)
                v.drain()
                self.mark(s_xv, v.engine_nop(), f"sacc{m}")
        self.close_scope(joins=[(s_xv, s_xv.n), (s_xp, s_xp.n), (s_xg, s_xg.n),
                                (s_xt, s_xt.n)])

    def ph_bsld(self):
        blk = self.blk
        sc = self.open_scope()
        pXb = self._scope.enter_context(self.nc.psum_tensor("pXb", [128, 512], F32))
        s_bt = self.sem("s_bt")
        for b in range(8):
            @blk.tensor
            def _(te, b=b):
                self.apply_joins(te, "bs0")
                if b == 0:
                    te.wait_ge(self.s_dtdone.h, self.s_dtdone.n)
                    te.wait_ge(self.s_ld.h, self.LD0)
                else:
                    self.waitm(te, f"bsl{b - 1}")
                te.transpose(pXb[:, 0:128], self.dta[:, 4, 128 * b:128 * (b + 1)],
                             self.sidentf[:, :])
                self.mark(s_bt, te.transpose(pXb[:, 128:256],
                                             self.dta[:, 1, 128 * b:128 * (b + 1)],
                                             self.sidentf[:, :]), f"bslt{b}")

            @blk.vector
            def _(v, b=b):
                self.apply_joins(v, "bs0")
                self.waitm(v, f"bslt{b}")
                v.tensor_copy(self.bsT[:, b, :], pXb[:, 0:128])
                v.tensor_copy(self.ldT[:, b, :], pXb[:, 128:256])
                v.drain()
                self.mark(self.s_misc, v.engine_nop(), f"bsl{b}")
        self.close_scope(joins=[(s_bt, s_bt.n), (self.s_misc, self.s_misc.n)])
        self.dtsub_scope.close()

    def ph_ssd_y(self):
        blk, d = self.blk, self.d
        NP = 64
        import os as _os
        NCHE = int(_os.environ.get("SSDC", str(NCH)))
        NPE = int(_os.environ.get("SSDP", str(64)))
        LVL = int(_os.environ.get("SSDLVL", "6"))
        sc = self.open_scope()
        selh_s = self.ssb("selh_s", [128, 128, 128], BF16)
        hs_gc = [self.ssb(f"hs_gc{i}", [128, 16, 256], BF16) for i in range(2)]
        gt_gc = [self.ssb(f"gt_gc{i}", [128, 16, 256], BF16) for i in range(2)]
        prevc = self.ssb("prevc", [128, INTER], BF16)
        xcm = self.ssb("xcm", [128, 2, 2, 128], BF16)
        cbm = self.ssb("cbm", [128, 2, 256], BF16)
        ctl = self.ssb("ctl", [128, 2, 2, 256], BF16)
        ecb = self.ssb("ecb", [128, 2, 256], BF16)
        sarg = self.ssb("sarg", [128, 2, 512], F32)
        es = self.ssb("es", [128, 2, 512], BF16)
        sct = self.ssb("sct", [128, 2, 2, 512], BF16)
        sincp = self.ssb("sincp", [128, 2, 128], BF16)
        pfb = self.ssb("pfb", [128, 128], BF16)
        prow2 = self.ssb("prow2", [1, 128], BF16)
        ysb = self.ssb("ysb", [128, 256], BF16)
        gt = self.ssb("gt", [128, 256], BF16)
        gsq = self.ssb("gsq", [128, 2, 256], BF16)
        gbuf = [self.ssb(f"gbuf{i}", [128, 16, 256], BF16) for i in range(2)]
        rstc = self.ssb("rstc", [8, 256], F32)
        psc = self._scope.enter_context
        pcb = psc(self.nc.psum_tensor("pcb", [128, 2, 512], F32))
        pbc = psc(self.nc.psum_tensor("pbc", [128, 2, 512], F32))
        pY = psc(self.nc.psum_tensor("pY", [128, 2, 512], F32))
        pSS = psc(self.nc.psum_tensor("pSS", [8, 512], F32))
        pXs = psc(self.nc.psum_tensor("pXs", [128, 1024], BF16))
        s_yh = self.sem("s_yh")
        s_yt = self.sem("s_yt")
        s_ya = self.sem("s_ya")
        s_ymm = self.sem("s_ymm")

        @blk.sync
        def _(sy):
            self.apply_joins(sy, "sy0")
            self.mark(s_yh, sy.dma_start(out=selh_s[:, :, :], in_=d["selh"]),
                      "selhld", 16)

        def emit_evac(pi):
            c0, m0 = pi // NP, pi % NP

            @blk.vector
            def _(v, c=c0, m=m0, pi=pi):
                self.waitm(v, f"ymm{pi}")
                if pi == 0:
                    v.wait_ge(self.s_l2g.h, self.s_l2g.n)
                if pi >= 32:
                    self.waitm(v, f"gflush{(pi - 32) // 16}")
                g16l = (pi // 16) % 2
                v.scalar_tensor_tensor(ysb[:, :], hs_gc[g16l][:, m % 16, :],
                                       self.sdrep[:, m:m + 1], pY[:, m % 2, 0:256],
                                       ALU.mult, ALU.add)
                v.drain()
                v.tensor_tensor(gt[:, :], ysb[:, :], gt_gc[g16l][:, m % 16, :],
                                ALU.mult)
                v.drain()
                v.tensor_tensor(gsq[:, m % 2, :], gt[:, :], gt[:, :], ALU.mult)
                v.tensor_tensor(gbuf[(m // 16) % 2][:, m % 16, :], gt[:, :],
                                self.snw[:, m:m + 1].broadcast_to([128, 256]),
                                ALU.mult)
                v.drain()
                self.mark(self.s_misc, v.engine_nop(), f"yev{pi}")

        def emit_ss(pi):
            c0, m0 = pi // NP, pi % NP

            @blk.tensor
            def _(te, c=c0, m=m0, pi=pi):
                self.waitm(te, f"yev{pi}")
                if m == 0 and c > 0:
                    self.waitm(te, f"rstc{c - 1}")
                self.mark(s_ymm, te.matmul(pSS[0:8, 0:256],
                                           self.sbsel[:, m // 8, :], gsq[:, m % 2, :],
                                           start=(m == 0), stop=(m == NPE - 1)),
                          f"yss{pi}")

        for c in range(NCHE):
            @blk.sync
            def _(sy, c=c):
                if c > 0:
                    self.waitm(sy, f"ymm{c * NP - 1}")
                self.mark(s_yh, sy.dma_start(out=prevc[:, :], in_=d["prevD"][c, :, :]),
                          f"pvl{c}", 16)

            @blk.tensor
            def _(te, c=c):
                self.apply_joins(te, "sy0")
                if c == 0:
                    te.wait_ge(self.s_dtdone.h, self.s_dtdone.n)
                    te.wait_ge(self.s_ld.h, self.LD0)
                else:
                    self.waitm(te, f"cbmev{c * 8 - 1}")
                self.mark(s_yt, te.transpose(pcb[0:1, 0, 0:128], self.cl[:, 8 + c:9 + c],
                                             self.sidentf[:, :]), f"pfbt{c}")

            @blk.vector
            def _(v, c=c):
                self.apply_joins(v, "sy0")
                self.waitm(v, f"pfbt{c}")
                v.tensor_copy(prow2[:, :], pcb[0:1, 0, 0:128])
                v.drain()
                self.mark(self.s_misc, v.engine_nop(), f"prow2{c}")

            @blk.tensor
            def _(te, c=c):
                self.waitm(te, f"prow2{c}")
                self.mark(s_yt, te.matmul(pcb[:, 1, 0:128], self.sones1[:, :],
                                          prow2[:, :], start=True, stop=True), f"pfbm{c}")

            @blk.vector
            def _(v, c=c):
                self.waitm(v, f"pfbm{c}")
                v.tensor_copy(pfb[:, :], pcb[:, 1, 0:128])
                v.drain()
                self.mark(self.s_misc, v.engine_nop(), f"pfbev{c}")

            for m in range(NPE):
                g = m // 8
                gi = c * 8 + g
                pi = c * NP + m
                par = m % 2
                g16 = pi // 16
                if m % 16 == 0:
                    @blk.sync
                    def _(sy, c=c, m=m, pi=pi, g16=g16):
                        if pi >= 32:
                            q = (pi - 32) // 16
                            self.waitm(sy, f"yev{16 * q + 15}")
                            self.mark(s_yh, sy.dma_start(
                                out=d["gD"][q // 4, :, 16 * (q % 4):16 * (q % 4) + 16, :],
                                in_=gbuf[q % 2][:, :, :]), f"gflush{q}", 16)
                            self.waitm(sy, f"xcmt{pi - 17}_1")
                            self.waitm(sy, f"yev{pi - 17}")
                        s_yh.dma(sy.dma_start(out=hs_gc[g16 % 2][:, :, :],
                                              in_=d["hsD"][c, :, m:m + 16, :]))
                        self.mark(s_yh, sy.dma_start(out=gt_gc[g16 % 2][:, :, :],
                                                     in_=d["gateD"][c, :, m:m + 16, :]),
                                  f"hgl{g16}", 16)

                if m % 8 == 0 and LVL >= 2:
                    @blk.tensor
                    def _(te, c=c, g=g, gi=gi):
                        te.wait_ge(self.s_l2.h, self.L2LD)
                        self.waitm(te, f"pfbev{c}")
                        if gi > 0:
                            self.waitm(te, f"cbmev{gi - 1}")
                        for half in range(2):
                            self.mark(s_yt, te.matmul(
                                pcb[:, half, 0:256],
                                self.bc_ch[:, g, 256 * c + 128 * half:
                                            256 * c + 128 * (half + 1)],
                                self.bc_ch[:, 8 + g, 256 * c:256 * (c + 1)],
                                start=True, stop=True), f"cbt{gi}_{half}")

                    @blk.vector
                    def _(v, c=c, g=g, gi=gi):
                        for half in range(2):
                            self.waitm(v, f"cbt{gi}_{half}")
                            v.tensor_tensor(cbm[:, half, :], pcb[:, half, 0:256],
                                            self.stri[:, half, :], ALU.mult)
                        v.drain()
                        self.mark(self.s_misc, v.engine_nop(), f"cbmev{gi}")

                if LVL < 3:
                    continue
                @blk.tensor
                def _(te, c=c, m=m, pi=pi, g16=g16):
                    self.waitm(te, f"hgl{g16}")
                    if pi > 0:
                        self.waitm(te, f"xcme{pi - 1}")
                    for half in range(2):
                        self.mark(s_yt, te.transpose(
                            pXs[:, 128 * half:128 * (half + 1)],
                            hs_gc[g16 % 2][:, m % 16, 128 * half:128 * (half + 1)],
                            self.sident[:, :]), f"xcmt{pi}_{half}")

                @blk.vector
                def _(v, c=c, m=m, pi=pi, par=par):
                    if pi >= 2:
                        self.waitm(v, f"ymm{pi - 2}")
                    for half in range(2):
                        self.waitm(v, f"xcmt{pi}_{half}")
                        v.tensor_copy(xcm[:, par, half, :],
                                      pXs[:, 128 * half:128 * (half + 1)])
                    self.waitm(v, f"pfbev{c}")
                    v.tensor_tensor(
                        sincp[:, par, :].rearrange("p (s q) -> p s q", s=2),
                        self.sacc[:, 128 * m:128 * (m + 1)].rearrange(
                            "p (s q) -> p s q", s=2),
                        pfb[:, 2 * m:2 * m + 2].rearrange(
                            "p (s q) -> p s q", q=1).broadcast_to([128, 2, 64]),
                        ALU.mult)
                    v.drain()
                    self.mark(self.s_misc, v.engine_nop(), f"xcme{pi}")

                for sub in (range(2) if LVL >= 4 else ()):
                    h = 2 * m + sub
                    hb = c * 128 + h
                    hpar = h % 2

                    @blk.tensor
                    def _(te, c=c, h=h, hb=hb, hpar=hpar):
                        self.waitm(te, "selhld")
                        if hb >= 2:
                            self.waitm(te, f"sargd{hb - 2}")
                            self.waitm(te, f"ecb{hb - 2}")
                        last = None
                        for sp in range(3):
                            last = te.matmul(pbc[:, hpar, 0:256], selh_s[:, h, :],
                                             self.cum_bf[:, sp, 256 * c:256 * (c + 1)],
                                             start=(sp == 0), stop=(sp == 2))
                        self.mark(s_yt, last, f"bct{hb}")

                    @blk.vector
                    def _(v, c=c, h=h, hb=hb, hpar=hpar):
                        self.waitm(v, f"bct{hb}")
                        if hb >= 2:
                            self.waitm(v, f"es{hb - 2}")
                        for half in range(2):
                            blk8 = 2 * c + half
                            v.tensor_scalar(sarg[:, hpar, 256 * half:256 * (half + 1)],
                                            pbc[:, hpar, 0:256],
                                            self.bsT[:, blk8, h:h + 1],
                                            self.ldT[:, blk8, h:h + 1],
                                            ALU.add, ALU.min)
                        v.drain()
                        self.mark(self.s_misc, v.engine_nop(), f"sargd{hb}")

                    @blk.scalar
                    def _(s, hb=hb, hpar=hpar):
                        self.apply_joins(s, "sy0")
                        self.waitm(s, f"sargd{hb}")
                        if hb >= 2:
                            self.waitm(s, f"sct{hb - 2}")
                        self.mark(s_ya, s.activation(ecb[:, hpar, :],
                                                     pbc[:, hpar, 0:256], AF.Exp),
                                  f"ecb{hb}")

                    @blk.scalar
                    def _(s, hb=hb, hpar=hpar):
                        self.waitm(s, f"sargd{hb}")
                        if hb >= 2:
                            self.waitm(s, f"sct{hb - 2}")
                        self.mark(s_ya, s.activation(es[:, hpar, :], sarg[:, hpar, :],
                                                     AF.Exp), f"es{hb}")

                    @blk.vector
                    def _(v, c=c, g=g, hb=hb, sub=sub, par=par, hpar=hpar, pi=pi):
                        self.waitm(v, f"es{hb}")
                        self.waitm(v, f"ecb{hb}")
                        if pi >= 2:
                            self.waitm(v, f"ymm{pi - 2}")
                        v.tensor_tensor(sct[:, par, sub, :], es[:, hpar, :],
                                        cbm[:, :], ALU.mult)
                        v.tensor_tensor(ctl[:, par, sub, :],
                                        self.bc_ch[:, 8 + g,
                                                   256 * c:256 * (c + 1)],
                                        ecb[:, hpar, :], ALU.mult)
                        v.drain()
                        self.mark(self.s_misc, v.engine_nop(), f"sct{hb}")

                if LVL < 5:
                    continue
                @blk.tensor
                def _(te, c=c, m=m, pi=pi, par=par):
                    if pi >= 2:
                        self.waitm(te, f"yev{pi - 2}")
                    if m == 0:
                        self.waitm(te, f"pvl{c}")
                    self.waitm(te, f"xcme{pi}")
                    self.waitm(te, f"sct{c * 128 + 2 * m + 1}")
                    last = None
                    for sub in range(2):
                        r0 = 64 * sub
                        first = True
                        for half in range(2):
                            last = te.matmul(pY[r0:r0 + 64, par, 0:256],
                                             xcm[:, par, half, 64 * sub:64 * (sub + 1)],
                                             sct[:, par, sub, 256 * half:
                                                 256 * (half + 1)],
                                             start=first, stop=False)
                            first = False
                        last = te.matmul(pY[r0:r0 + 64, par, 0:256],
                                         prevc[:, 128 * m + 64 * sub:
                                               128 * m + 64 * (sub + 1)],
                                         ctl[:, par, sub, :], start=False, stop=False)
                        last = te.matmul(pY[r0:r0 + 64, par, 0:256],
                                         sincp[:, par, 64 * sub:64 * (sub + 1)],
                                         ctl[:, par, sub, :], start=False, stop=True)
                    self.mark(s_ymm, last, f"ymm{pi}")

                if m > 0 and LVL >= 6:
                    emit_evac(pi - 1)
                    emit_ss(pi - 1)

            if LVL >= 6:
                emit_evac(c * NP + NPE - 1)
                emit_ss(c * NP + NPE - 1)
            if NPE < NP or LVL < 6:
                continue

            @blk.scalar
            def _(s, c=c):
                self.waitm(s, f"yss{c * NP + NP - 1}")
                if c > 0:
                    self.waitm(s, f"ychv{c - 1}")
                s.activation(rstc[:, :], pSS[0:8, 0:256], AF.Ln,
                             bias=self.cb1[0:8, 0:1], scale=1.0 / 1024.0)
                s.drain()
                self.mark(s_ya, s.activation(rstc[:, :], rstc[:, :], AF.Exp,
                                             scale=-0.5), f"rstc{c}")

            @blk.vector
            def _(v, c=c):
                self.waitm(v, f"rstc{c}")
                v.tensor_copy(self.rstP[:, c, :], rstc[:, :])
                v.drain()
                self.mark(self.s_misc, v.engine_nop(), f"ychv{c}")

        for q in ((4 * NCHE - 2, 4 * NCHE - 1) if NPE == NP else ()):
            @blk.sync
            def _(sy, q=q):
                self.waitm(sy, f"yev{16 * q + 15}")
                self.mark(s_yh, sy.dma_start(
                    out=d["gD"][q // 4, :, 16 * (q % 4):16 * (q % 4) + 16, :],
                    in_=gbuf[q % 2][:, :, :]), f"gflush{q}", 16)
        self.close_scope(joins=[(s_ymm, s_ymm.n), (s_yt, s_yt.n), (s_ya, s_ya.n),
                                (self.s_misc, self.s_misc.n), (s_yh, s_yh.n)])
        self.mid_scope.close()

    def ph_ffn(self):
        blk, d = self.blk, self.d
        sc = self.open_scope()
        xnf = self.ssb("xnf", [128, 32, T], BF16)
        wfb = [self.ssb(f"wfb{i}", [128, 32, 512], BF16) for i in range(2)]
        fbs = self.ssb("fbs", [128, 4096], F32)
        iv = self.ssb("iv", [128, 8], F32)
        s1 = self.ssb("s1f", [128, 256], F32)
        s2 = self.ssb("s2f", [128, 256], F32)
        xfb = [self.ssb(f"xfb{i}", [128, 8, 256], F32) for i in range(2)]
        fstage = [self.ssb(f"fstage{i}", [128, 8, 256], F32) for i in range(2)]
        pF = self._scope.enter_context(self.nc.psum_tensor("pF", [128, 2, 512], F32))
        s_fw = self.sem("s_fw")
        s_fm = self.sem("s_fm")
        s_fe = self.sem("s_fe")
        s_fo = self.sem("s_fo")
        self.s_fo = s_fo

        @blk.sync
        def _(sy):
            self.apply_joins(sy, "ff0")
            s_fw.dma(sy.dma_start(out=fbs[:, :], in_=d["fb"]))
            self.mark(s_fw, sy.dma_start(out=xnf[:, :, :],
                                         in_=d["xn_chD"][:, :, 3:3 + T]), "xnld", 16)

        @blk.vector
        def _(v):
            self.apply_joins(v, "ff0")
            v.wait_ge(self.s_l2.h, self.L2LD)
            self.mark(self.s_misc, v.tensor_scalar(iv[:, :], self.mask8[:, :], -1.0, 1.0,
                                                   ALU.mult, ALU.add), "iv")

        for oc in range(16):
            @blk.sync
            def _(sy, oc=oc):
                if oc >= 2:
                    sy.wait_ge(s_fm.h, 8 * (oc - 1))
                self.mark(s_fw, sy.dma_start(out=wfb[oc % 2][:, :, :],
                                             in_=d["wlhT"][oc]), f"wfl{oc}", 16)
                if oc >= 2:
                    sy.wait_ge(s_fe.h, 9 * (oc - 1))
                self.mark(s_fw, sy.dma_start(
                    out=xfb[oc % 2][:, :, :],
                    in_=d["xh"][3:3 + T, 256 * oc:256 * (oc + 1)].rearrange(
                        "(t p) i -> p t i", p=128)), f"xfl{oc}", 16)

            @blk.tensor
            def _(te, oc=oc):
                self.apply_joins(te, "ff0")
                self.waitm(te, f"wfl{oc}")
                self.waitm(te, "xnld")
                for t in range(8):
                    gidx = oc * 8 + t
                    p2 = gidx % 2
                    if gidx >= 2:
                        oc2, t2 = (gidx - 2) // 8, (gidx - 2) % 8
                        te.wait_ge(s_fe.h, oc2 * 9 + t2 + 1)
                    last = None
                    for k in range(32):
                        last = te.matmul(pF[:, p2, :],
                                         xnf[:, k, 128 * t:128 * (t + 1)],
                                         wfb[oc % 2][:, k, :],
                                         start=(k == 0), stop=(k == 31))
                    s_fm.inc(last)

            @blk.vector
            def _(v, oc=oc):
                self.waitm(v, "iv")
                self.waitm(v, f"xfl{oc}")
                for t in range(8):
                    p2 = (oc * 8 + t) % 2
                    v.wait_ge(s_fm.h, oc * 8 + t + 1)
                    if oc >= 2 and t == 0:
                        v.wait_ge(s_fo.h, 16 * (oc - 1))
                    v.scalar_tensor_tensor(s1[:, :], pF[:, p2, 0:256], iv[:, t:t + 1],
                                           fbs[:, 256 * oc:256 * (oc + 1)],
                                           ALU.mult, ALU.add)
                    v.drain()
                    v.scalar_tensor_tensor(s2[:, :], pF[:, p2, 256:512],
                                           self.mask8[:, t:t + 1], s1[:, :],
                                           ALU.mult, ALU.add)
                    v.drain()
                    inst = v.tensor_tensor(fstage[oc % 2][:, t, :], s2[:, :],
                                           xfb[oc % 2][:, t, :], ALU.add)
                    s_fe.inc(inst)
                v.drain()
                s_fe.inc(v.engine_nop())

            @blk.sync
            def _(sy, oc=oc):
                sy.wait_ge(s_fe.h, (oc + 1) * 9)
                s_fo.dma(sy.dma_start(
                    out=d["ffnD"][:, :, 256 * oc:256 * (oc + 1)].rearrange(
                        "t p i -> p t i"),
                    in_=fstage[oc % 2][:, :, :]))
        self.close_scope(joins=[(s_fo, s_fo.n), (s_fe, s_fe.n), (s_fm, s_fm.n),
                                (s_fw, s_fw.n)])

    def ph_outproj_final(self):
        blk, d = self.blk, self.d
        sc = self.open_scope()
        g_sb = self.ssb("g_sb", [128, 64, 512], BF16)
        wob = [self.ssb(f"wob{i}", [128, 32, 256], BF16) for i in range(2)]
        xpf = self.ssb("xpf", [128, 4, 4096], F32)
        rstB = self.ssb("rstB", [128, 8, 512], BF16)
        stg = self.ssb("stg", [128, 2, 256], F32)
        psc = self._scope.enter_context
        pG = psc(self.nc.psum_tensor("pG", [128, 2, 512], F32))
        s_ow = self.sem("s_ow")
        s_og = self.sem("s_og")
        s_om = self.sem("s_om")

        for hf in range(2):
            @blk.sync
            def _(sy, hf=hf):
                self.apply_joins(sy, "op0")
                self.waitm(sy, f"gflush{8 * hf + 7}")
                if hf > 0:
                    self.waitm(sy, "opg63")
                for cc in range(2):
                    self.mark(s_og, sy.dma_start(
                        out=g_sb[:, :, 256 * cc:256 * (cc + 1)],
                        in_=d["gD"][2 * hf + cc, :, :, :]), f"gld{hf}", 16)
                sy.wait_ge(self.s_fo.h, 16 * 16)
                if hf > 0:
                    self.waitm(sy, f"oev{63}")
                for ts in range(4):
                    self.mark(s_og, sy.dma_start(out=xpf[:, ts, :],
                                                 in_=d["ffnD"][4 * hf + ts, :, :]),
                              f"xpl{hf}", 16)

            for j in range(16):
                gg, c2 = j % 8, j // 8

                @blk.tensor
                def _(te, hf=hf, j=j, gg=gg, c2=c2):
                    self.apply_joins(te, "op0")
                    if hf == 0 and j == 0:
                        te.wait_ge(self.s_l2.h, self.L2LD)
                    self.waitm(te, f"ychv{2 * hf + c2}")
                    if j >= 2:
                        self.waitm(te, f"rbe{hf}_{j - 2}")
                    if hf > 0 and j < 2:
                        self.waitm(te, f"oev{62 + j}")
                    self.mark(s_om, te.matmul(pG[:, j % 2, 0:256],
                                              self.g8s[:, gg, :],
                                              self.rstP[:, 2 * hf + c2, :],
                                              start=True, stop=True), f"rbm{hf}_{j}")

                @blk.vector
                def _(v, hf=hf, j=j, gg=gg, c2=c2):
                    self.apply_joins(v, "op0")
                    self.waitm(v, f"rbm{hf}_{j}")
                    v.tensor_copy(rstB[:, gg, 256 * c2:256 * (c2 + 1)],
                                  pG[:, j % 2, 0:256])
                    v.drain()
                    self.mark(self.s_misc, v.engine_nop(), f"rbe{hf}_{j}")

            @blk.vector
            def _(v, hf=hf):
                self.waitm(v, f"rbe{hf}_{15}")
                self.waitm(v, f"gld{hf}")
                for m in range(64):
                    v.tensor_tensor(g_sb[:, m, :], g_sb[:, m, :],
                                    rstB[:, m // 8, :], ALU.mult)
                v.drain()
                self.mark(self.s_misc, v.engine_nop(), f"gsc{hf}")

            for sl in range(16):
                i = hf * 16 + sl

                @blk.gpsimd
                def _(g, hf=hf, sl=sl, i=i):
                    self.apply_joins(g, "op0")
                    if i > 0:
                        self.waitm(g, f"opg{4 * (i - 1) + 3}")
                    for half in range(2):
                        s_ow.dma(g.dma_start(
                            out=wob[half][:, :, :],
                            in_=d["wTo"][sl, :, 32 * half:32 * (half + 1), :]))

                for ts in range(4):
                    jg = 4 * i + ts

                    @blk.tensor
                    def _(te, hf=hf, sl=sl, i=i, ts=ts, jg=jg):
                        te.wait_ge(s_ow.h, 32 * (i + 1))
                        self.waitm(te, f"gsc{hf}")
                        if jg >= 2:
                            self.waitm(te, f"oev{jg - 2}")
                        last = None
                        halves = (0, 1) if ts < 3 else (1, 0)
                        first = True
                        for half in halves:
                            for mh in range(32):
                                last = te.matmul(
                                    pG[:, jg % 2, 0:256],
                                    g_sb[:, 32 * half + mh, 128 * ts:128 * (ts + 1)],
                                    wob[half][:, mh, :],
                                    start=first, stop=(half == halves[1] and mh == 31))
                                first = False
                        self.mark(s_om, last, f"opg{jg}")

                    @blk.vector
                    def _(v, hf=hf, sl=sl, i=i, ts=ts, jg=jg):
                        self.waitm(v, f"opg{jg}")
                        self.waitm(v, f"xpl{hf}")
                        if jg >= 2:
                            self.waitm(v, f"ost{jg - 2}")
                        v.tensor_tensor(stg[:, jg % 2, :], pG[:, jg % 2, 0:256],
                                        xpf[:, ts, 256 * sl:256 * (sl + 1)], ALU.add)
                        v.drain()
                        self.mark(self.s_misc, v.engine_nop(), f"oev{jg}")

                    @blk.sync
                    def _(sy, hf=hf, sl=sl, ts=ts, jg=jg):
                        self.waitm(sy, f"oev{jg}")
                        self.mark(self.s_end,
                                  sy.dma_start(out=d["out"][512 * hf + 128 * ts:
                                                            512 * hf + 128 * (ts + 1),
                                                            256 * sl:256 * (sl + 1)],
                                               in_=stg[:, jg % 2, :]), f"ost{jg}", 16)
        self.close_scope(joins=[(s_om, s_om.n), (self.s_misc, self.s_misc.n),
                                (s_og, s_og.n), (s_ow, s_ow.n)])



_COMMON_SHARED = ["identb", "identf", "ones1", "dtb", "alog", "zer"]
_L2_SHARED = ["wTo", "selh", "tri01", "Bsel8", "g8sel", "drepT", "nw", "fb"]


def _l1_inmap(shared, core):
    m = {k: shared[k] for k in _COMMON_SHARED + ["wTa", "shv", "plv", "off8", "off4",
                                                 "cw", "cb"]}
    m["qrep"] = core["qrep"]
    m["ltbl"] = core["ltbl"]
    m["xh"] = core["xh"]
    return m


def _l2_inmap(shared, core, r1r, states, wlhT):
    m = {k: shared[k] for k in _COMMON_SHARED + _L2_SHARED}
    m["selAe"] = core["selAe"]
    m["selBf"] = core["selBf"]
    m["xh"] = core["xh"]
    m["stateA"] = states
    m["wlhT"] = wlhT
    for k in ["xn_chD", "maskD", "gateD", "hsD", "bcD", "dtrawD", "prevD"]:
        m[k] = r1r[k]
    return m


def kernel(L0_attn_input_states, L0_attn_in_proj_w, L0_attn_conv_w, L0_attn_conv_b,
           L0_attn_dt_bias, L0_attn_A_log, L0_attn_D, L0_attn_norm_w, L0_attn_out_w,
           L0_ffn_qweight, L0_ffn_lut4, L0_ffn_lut8, L0_ffn_bias):
    inputs = dict(
        L0_attn_input_states=L0_attn_input_states, L0_attn_in_proj_w=L0_attn_in_proj_w,
        L0_attn_conv_w=L0_attn_conv_w, L0_attn_conv_b=L0_attn_conv_b,
        L0_attn_dt_bias=L0_attn_dt_bias, L0_attn_A_log=L0_attn_A_log,
        L0_attn_D=L0_attn_D, L0_attn_norm_w=L0_attn_norm_w, L0_attn_out_w=L0_attn_out_w,
        L0_ffn_qweight=L0_ffn_qweight, L0_ffn_lut4=L0_ffn_lut4, L0_ffn_lut8=L0_ffn_lut8,
        L0_ffn_bias=L0_ffn_bias)
    shared = _prep_shared(inputs)
    cores = [_prep_core(inputs, r) for r in range(NCORES)]

    nc1 = Prog("l1").build()
    res1 = run_bass_kernel_spmd(nc1, [_l1_inmap(shared, cores[r]) for r in range(NCORES)],
                                core_ids=list(range(NCORES)))
    r1 = res1.results

    states = np.stack([np.asarray(r1[r]["stateD"]) for r in range(NCORES)], 0)
    wl = np.concatenate([np.asarray(r1[r]["wlD"]) for r in range(NCORES)], 0)
    wh = np.concatenate([np.asarray(r1[r]["whD"]) for r in range(NCORES)], 0)
    wlt = wl.reshape(16, 256, 32, 128).transpose(0, 3, 2, 1)
    wht = wh.reshape(16, 256, 32, 128).transpose(0, 3, 2, 1)
    wlhT = np.ascontiguousarray(np.concatenate([wlt, wht], axis=3))

    nc2 = Prog("l2").build()
    maps2 = [_l2_inmap(shared, cores[r], r1[r], states, wlhT) for r in range(NCORES)]
    res2 = run_bass_kernel_spmd(nc2, maps2, core_ids=list(range(NCORES)))
    outs = [res2.results[r]["out"] for r in range(NCORES)]
    full = np.concatenate(outs, 0).reshape(2, 4096, 4096)
    return full
